# version 04
# baseline (speedup 1.0000x reference)
import os
os.environ.setdefault("TRN_ALLOW_ICI_ALL_TO_ALL", "1")

import numpy as np
from contextlib import ExitStack

import concourse.bass as bass
import concourse.mybir as mybir
from concourse.bass_utils import run_bass_kernel_spmd

F32 = mybir.dt.float32
BF16 = mybir.dt.bfloat16
AF = mybir.ActivationFunctionType
ALU = mybir.AluOpType
AX = mybir.AxisListType

NC = 8
D = 2048
DC = 16
TPC = 1024
LT = 2
TOK = 512
NB = 2
NST = 8
DTR = 256
CH = 256
HB = DTR + 16
QT = 1024
EPS = 1e-5


def build_program():
    TRUNC = int(os.environ.get("KTRUNC", "9"))
    nc = bass.Bass("TRN2", target_bir_lowering=False)

    def din(name, shape, dt=F32):
        return nc.dram_tensor(name, shape, dt, kind="ExternalInput").ap()

    x0_d = din("x0", [DC, 128, TPC])
    wb_d = din("wbT", [DC, 128, D])
    bb_d = din("bb", [1, D])
    rt_d = din("routerT", [DC, 128, NST])
    at_d = din("AT", [DC, 128, 128])
    bw_d = din("BwT", [128, D])
    wx_d = din("WxpT", [DC, 128, 384])
    wdt_d = din("WdtT", [2, 128, CH])
    bdt_d = din("bdt", [128, 2])
    alog_d = din("Alog", [128, 2, NST])
    dpar_d = din("Dpar", [128, DC])
    ones_col_d = din("ones_col", [128, 128])
    ones_row_d = din("ones_row", [1, TOK])
    eye_d = din("eye", [128, 128])
    e16_d = din("E16", [NST, 128])

    out_d = nc.dram_tensor("out", [DC, 128, TPC], F32, kind="ExternalOutput").ap()

    ag_inT = nc.dram_tensor("ag_inT", [NB, HB, TOK], BF16, kind="Internal").ap()
    ag_outT = nc.dram_tensor("ag_outT", [NC, NB, HB, TOK], BF16, kind="Internal").ap()
    a1_inT = nc.dram_tensor("a1_inT", [NC, NB, CH, TOK], F32, kind="Internal").ap()
    a1_outT = nc.dram_tensor("a1_outT", [NC, NB, CH, TOK], F32, kind="Internal").ap()
    a2_inT = nc.dram_tensor("a2_inT", [NC, NB, CH, TOK], F32, kind="Internal").ap()
    a2_outT = nc.dram_tensor("a2_outT", [NC, NB, CH, TOK], F32, kind="Internal").ap()
    ag_in = [ag_inT[b] for b in range(NB)]
    ag_out = [ag_outT[:, b] for b in range(NB)]
    a1_in = [a1_inT[:, b] for b in range(NB)]
    a1_out = [a1_outT[:, b] for b in range(NB)]
    a2_in = [a2_inT[:, b] for b in range(NB)]
    a2_out = [a2_outT[:, b] for b in range(NB)]

    T, V, S, G, Q = nc.tensor, nc.vector, nc.scalar, nc.gpsimd, nc.sync
    ENG = {"t": T, "v": V, "s": S, "g": G, "q": Q}

    es = ExitStack()
    sb = lambda n, s, d=F32: es.enter_context(nc.sbuf_tensor(n, s, d))

    with es:
        X = sb("X", [128, DC, TPC])
        rt = sb("rt", [128, DC, NST], BF16)
        wdt = sb("wdt", [128, 2, CH], BF16)
        bdt = sb("bdt_s", [128, 2])
        nbdt = sb("nbdt", [128, 2])
        aneg = sb("aneg", [128, 2, NST])
        dpar = sb("dpar_s", [128, DC])
        bbr = sb("bbr", [1, D], BF16)
        ones_c = sb("ones_c", [128, 128], BF16)
        onesf = sb("onesf", [128, 128])
        ones_r = sb("ones_r", [1, TOK], BF16)
        eye_b = sb("eye_b", [128, 128], BF16)
        e16 = sb("e16_s", [NST, 128], BF16)
        lnt = {}
        stt = sb("stt", [128, 16])

        ps8 = es.enter_context(nc.psum_tensor("ps8", [128, 8, 512], F32))

        OWN = {}

        class Ctr:
            def __init__(self, name):
                self.sem = es.enter_context(nc.semaphore(name))
                self.val = 0
        bs = {e: Ctr(f"bar_{e}") for e in ENG}
        dq = Ctr("dq"); dg = Ctr("dg"); ds = Ctr("ds")
        smm = Ctr("smm"); svv = Ctr("svv"); sss = Ctr("sss"); sgg = Ctr("sgg")
        scc = Ctr("scc")
        DQ = {"q": dq, "g": dg, "s": ds}
        OWN.update({"t": smm, "v": svv, "s": sss, "g": sgg})

        def bar():
            for e in ENG:
                bs[e].val += 1
            for e, eng in ENG.items():
                if e in DQ:
                    eng.wait_ge(DQ[e].sem, DQ[e].val)
                if e in OWN:
                    eng.wait_ge(OWN[e].sem, OWN[e].val)
                eng.sem_inc(bs[e].sem, 1)
            for e, eng in ENG.items():
                for o in ENG:
                    if o != e:
                        eng.wait_ge(bs[o].sem, bs[o].val)

        def dma(k, out, in_):
            c = DQ[k]
            if k == "g":
                ENG[k].wait_ge(c.sem, c.val)
            c.val += 16
            ENG[k].dma_start(out=out, in_=in_).then_inc(c.sem, 16)

        def mark(ctr, eng):
            eng.nop().then_inc(ctr.sem, 1)
            ctr.val += 1

        def imark(ctr, inst):
            inst.then_inc(ctr.sem, 1)
            ctr.val += 1

        def schain(ctr, eng, inst):
            imark(ctr, inst)
            eng.wait_ge(ctr.sem, ctr.val)

        with nc.Block():
            dma("q", X[:], x0_d.rearrange("c p t -> p c t"))
            dma("g", rt[:], rt_d.rearrange("c p e -> p c e"))
            dma("g", wdt[:], wdt_d.rearrange("c p m -> p c m"))
            dma("g", bbr[:], bb_d)
            dma("g", ones_c[:], ones_col_d)
            dma("q", onesf[:], ones_col_d)
            dma("g", ones_r[:], ones_row_d)
            dma("g", eye_b[:], eye_d)
            dma("g", e16[:], e16_d)
            dma("q", bdt[:], bdt_d)
            dma("q", dpar[:], dpar_d)
            dma("q", aneg[:], alog_d)
            S.wait_ge(dq.sem, dq.val)
            imark(sss, S.activation(aneg[:], aneg[:], AF.Exp))
            V.wait_ge(sss.sem, sss.val)
            V.tensor_scalar(aneg[:], aneg[:], -1.0, None, ALU.mult)
            imark(svv, V.tensor_scalar(nbdt[:], bdt[:], -1.0, None, ALU.mult))
            bar()

        def ln_tile(j, dst_bf, extra_f32=None):
            sq_lt = dst_bf[:, :, j * TOK:(j + 1) * TOK]
            mbc, rbc, vtm, scr = lnt["mbc"], lnt["rbc"], lnt["vtm"], lnt["scr"]
            psSx = ps8[:, 4, :]
            psSq = ps8[:, 5, :]
            V.wait_ge(dq.sem, dq.val)
            V.wait_ge(svv.sem, svv.val)
            imark(sss, S.activation(sq_lt, X[:, :, j * TOK:(j + 1) * TOK], AF.Square))
            T.wait_ge(svv.sem, svv.val)
            T.wait_ge(sss.sem, sss.val)
            for kt in range(DC):
                T.matmul(psSx, onesf[:], X[:, kt, j * TOK:(j + 1) * TOK],
                         start=(kt == 0), stop=(kt == DC - 1))
            for kt in range(DC):
                mi = T.matmul(psSq, ones_c[:], sq_lt[:, kt, :],
                              start=(kt == 0), stop=(kt == DC - 1))
            imark(smm, mi)
            S.wait_ge(smm.sem, smm.val)
            S.activation(mbc[:], psSx, AF.Copy, scale=1.0 / D)
            S.activation(vtm[:], psSx, AF.Square, scale=1.0 / D)
            imark(sss, S.activation(scr[:], psSq, AF.Copy, scale=1.0 / D))
            V.wait_ge(sss.sem, sss.val)
            imark(svv, V.scalar_tensor_tensor(rbc[:], scr[:], EPS, vtm[:],
                                              ALU.add, ALU.subtract))
            T.wait_ge(svv.sem, svv.val)
            S.wait_ge(svv.sem, svv.val)
            schain(sss, S, S.activation(rbc[:], rbc[:], AF.Ln))
            imark(sss, S.activation(rbc[:], rbc[:], AF.Exp, scale=-0.5))
            V.wait_ge(sss.sem, sss.val)
            for dc in range(DC):
                V.wait_ge(svv.sem, svv.val)
                schain(svv, V, V.tensor_tensor(scr[:], X[:, dc, j * TOK:(j + 1) * TOK],
                                               mbc[:], ALU.subtract))
                mi = V.tensor_tensor(dst_bf[:, dc, j * TOK:(j + 1) * TOK], scr[:],
                                     rbc[:], ALU.mult)
                if extra_f32 is not None:
                    mi = V.tensor_tensor(extra_f32[:, dc, :], scr[:], rbc[:], ALU.mult)
                imark(svv, mi)
            S.wait_ge(svv.sem, svv.val)

        def ffn_sublayer(tn):
            SUB = int(os.environ.get("KSUB", "9"))
            xlnb, m8st = tn["xlnb"], tn["m8st"]
            logsb = tn["logsb"]
            mtok, mbuf, hmk, mx8, wbh = (tn["mtok"], tn["mbuf"],
                                         tn["hmk"], tn["mx8"], tn["wbh"])
            psLog = ps8[:, 4, 0:NST]
            psT8 = ps8[0:NST, 5, 0:64].bitcast(BF16)
            psT8w = ps8[:, 5, 0:64]
            psH = ps8[:, 6, :]
            psM = ps8[:, 7, :]
            for j in (range(LT) if SUB != 40 else []):
                ln_tile(j, xlnb)
            bar()
            for j in (range(LT) if (SUB >= 2 and SUB != 40) else []):
                for half in range(4):
                    t0 = j * TOK + half * 128
                    for kt in range(DC):
                        mi = T.matmul(psLog, xlnb[:, kt, t0:t0 + 128],
                                      rt[:, kt, :], start=(kt == 0), stop=(kt == DC - 1))
                    imark(smm, mi)
                    S.wait_ge(smm.sem, smm.val)
                    S.wait_ge(svv.sem, svv.val)
                    imark(sss, S.copy(logsb[:], psLog))
                    T.wait_ge(sss.sem, sss.val)
                    V.wait_ge(sss.sem, sss.val)
                    schain(svv, V, V.reduce_max(mx8[:], logsb[:], axis=AX.X))
                    imark(svv, V.tensor_scalar(mtok[:, j, half, :], logsb[:], mx8[:], 2.0,
                                               ALU.is_equal, ALU.mult))
            bar()
            for j in (range(LT) if (SUB >= 3 and SUB != 40) else []):
                for half in range(4):
                    imark(smm, T.matmul(psT8, mtok[:, j, half, :], eye_b[:],
                                        is_transpose=True, start=True, stop=True))
                    S.wait_ge(smm.sem, smm.val)
                    imark(sss, S.copy(m8st[:, half, :], psT8w))
                    T.wait_ge(sss.sem, sss.val)
                imark(smm, T.matmul(psM, e16[:],
                      m8st[0:NST, :, :].bitcast(BF16).rearrange("p h t -> p (h t)"),
                      start=True, stop=True))
                S.wait_ge(smm.sem, smm.val)
                S.wait_ge(svv.sem, svv.val)
                imark(sss, S.copy(mbuf[:], psM))
                T.wait_ge(sss.sem, sss.val)
                for kt in range(DC):
                    mi = T.matmul(psH, tn["at"][:, kt, :],
                                  xlnb[:, kt, j * TOK:(j + 1) * TOK],
                                  start=(kt == 0), stop=(kt == DC - 1))
                imark(smm, mi)
                V.wait_ge(smm.sem, smm.val)
                V.wait_ge(sss.sem, sss.val)
                imark(svv, V.tensor_tensor(hmk[:, j, :], psH, mbuf[:], ALU.mult))
                T.wait_ge(svv.sem, svv.val)
            bar()
            wpar = tn["wb2"]
    
            ep_marks = []
    
            ld_marks = []
            g_idx = 0
            for h in range(4):
                if h >= 1:
                    pass
                for j in range(LT):
                    if h == 0 and j == 0:
                        dma("g", wpar[1][:],
                            wb_d.rearrange("c p o -> p c o")[:, :, 512:1024])
                        ld_marks.append(dg.val)
                    if h >= 1 and j == 0:
                        T.wait_ge(dg.sem, ld_marks[h - 1])
                        if h < 3:
                            G.wait_ge(smm.sem, smm.val)
                            dma("g", wpar[(h + 1) % 2][:],
                                wb_d.rearrange("c p o -> p c o")[:, :, (h + 1) * 512:(h + 2) * 512])
                            ld_marks.append(dg.val)
                    wcur = wpar[h % 2]
                    b0 = (g_idx % 2) * 4
                    if g_idx >= 2:
                        T.wait_ge(svv.sem, ep_marks[g_idx - 2])
                    for i in range(4):
                        og = h * 4 + i
                        psO = ps8[:, b0 + i, :]
                        for kt in range(DC):
                            T.matmul(psO, wcur[:, kt, i * 128:(i + 1) * 128],
                                     xlnb[:, kt, j * TOK:(j + 1) * TOK],
                                     start=(kt == 0), stop=False)
                        T.matmul(psO, bbr[0:1, og * 128:(og + 1) * 128],
                                 ones_r[:], start=False, stop=False)
                        mi = T.matmul(psO, tn["bw"][:, og * 128:(og + 1) * 128],
                                      hmk[:, j, :], start=False, stop=True)
                    imark(smm, mi)
                    V.wait_ge(smm.sem, smm.val)
                    o0 = h * 4
                    imark(svv, V.scalar_tensor_tensor(
                        X[:, o0:o0 + 4, j * TOK:(j + 1) * TOK],
                        ps8[:, b0:b0 + 4, :], 0.5,
                        X[:, o0:o0 + 4, j * TOK:(j + 1) * TOK],
                        ALU.mult, ALU.add))
                    ep_marks.append(svv.val)
                    g_idx += 1
            bar()

        es1 = ExitStack()
        sb1 = lambda n, s, d=F32: es1.enter_context(nc.sbuf_tensor(n, s, d))
        with es1:
            tn1 = dict(
                wbh=sb1("wbh", [128, DC, 512], BF16),
                wb2=None,
                xlnb=sb1("xlnb", [128, DC, TPC], BF16),
                mtok=sb1("mtok", [128, LT, 4, NST], BF16),
                m8st=sb1("m8st", [128, 4, 64]),
                mbuf=sb1("mbuf", [128, TOK], BF16),
                hmk=sb1("hmk", [128, LT, TOK], BF16),
                mx8=sb1("mx8", [128, 1]),
                logsb=sb1("logsb", [128, NST]),
            )
            tn1["wbhB"] = sb1("wbhB", [128, DC, 512], BF16)
            tn1["wb2"] = [tn1["wbh"], tn1["wbhB"]]
            xln2 = sb1("xln2", [128, DC, TOK])
            xdbe = sb1("xdbe", [128, 3, TOK], BF16)
            wx = sb1("wx", [128, DC, 384], BF16)
            tn1["at"] = sb1("at", [128, DC, 128], BF16)
            tn1["bw"] = sb1("bw", [128, D], BF16)
            for nm in ("mbc", "rbc", "vtm", "scr"):
                lnt[nm] = sb1(nm, [128, TOK])

            with nc.Block():
                dma("g", tn1["wbh"][:],
                    wb_d.rearrange("c p o -> p c o")[:, :, 0:512])
                dma("g", wx[:], wx_d.rearrange("c p k -> p c k"))
                dma("g", tn1["at"][:], at_d.rearrange("c p e -> p c e"))
                dma("g", tn1["bw"][:], bw_d)
                bar()
                if TRUNC >= 2:
                    ffn_sublayer(tn1)
                for j in (range(LT) if TRUNC >= 3 else []):
                    b = j
                    ln_tile(j, tn1["xlnb"], extra_f32=xln2)
                    Q.wait_ge(svv.sem, svv.val)
                    for g8 in range(NC):
                        dma("q", a1_in[b][g8, :, :].rearrange("(c p) t -> p c t", p=128),
                            xln2[:, 2 * g8:2 * g8 + 2, :])
                    V.wait_ge(svv.sem, svv.val)
                    for dc in range(DC):
                        mi = V.scalar_tensor_tensor(X[:, dc, j * TOK:(j + 1) * TOK],
                                                    xln2[:, dc, :], dpar[:, dc:dc + 1],
                                                    X[:, dc, j * TOK:(j + 1) * TOK],
                                                    ALU.mult, ALU.add)
                    imark(svv, mi)
                    T.wait_ge(svv.sem, svv.val)
                    for ot in (0, 2, 1):
                        psXo = ps8[:, 4 + ot, :]
                        for kt in range(DC):
                            mi = T.matmul(psXo, wx[:, kt, ot * 128:(ot + 1) * 128],
                                          tn1["xlnb"][:, kt, j * TOK:(j + 1) * TOK],
                                          start=(kt == 0), stop=(kt == DC - 1))
                    imark(smm, mi)
                    S.wait_ge(smm.sem, smm.val)
                    imark(sss, S.copy(xdbe[:], ps8[:, 4:7, :]))
                    T.wait_ge(sss.sem, sss.val)
                    Q.wait_ge(sss.sem, sss.val)
                    dma("q", ag_in[b][0:DTR, :].rearrange("(c p) t -> p c t", p=128),
                        xdbe[:, 0:2, :])
                    dma("q", ag_in[b][DTR:HB, :], xdbe[0:16, 2, :])
                    V.wait_ge(dq.sem, dq.val)
                    S.wait_ge(dq.sem, dq.val)
                bar()

        es2 = ExitStack()
        sb2 = lambda n, s, d=F32: es2.enter_context(nc.sbuf_tensor(n, s, d))
        with es2:
            x2ch = [sb2(f"x2ch{p}", [128, 2, QT]) for p in range(2)]
            dtc = [sb2(f"dtc{p}", [128, 2, QT], BF16) for p in range(2)]
            dlt = [sb2(f"dlt{p}", [128, 2, QT]) for p in range(2)]
            dab = [sb2(f"dab{p}", [128, QT]) for p in range(2)]
            dbu = sb2("dbu", [128, QT])
            hbt = [sb2(f"hbt{p}", [128, QT]) for p in range(2)]
            hcq = sb2("hcq", [128, QT, NST], BF16)
            ysc = sb2("ysc", [128, 2, QT])
            bq = [sb2(f"bq{p}", [128, NST, QT], BF16) for p in range(2)]
            cq = sb2("cq", [128, NST, QT], BF16)

            with nc.Block():
                if TRUNC >= 4:
                    imark(scc, G.collective_compute("AllGather", ALU.bypass,
                                                    ins=[ag_inT], outs=[ag_outT],
                                                    replica_groups=[list(range(NC))]))
                    imark(scc, G.collective_compute("AllToAll", ALU.bypass,
                                                    ins=[a1_inT], outs=[a1_outT],
                                                    replica_groups=[list(range(NC))]))
                G.wait_ge(scc.sem, scc.val)
                bar()

                NCH = 8 if TRUNC >= 5 else 0
                ld_mark = {}
                cq_mark = {}
                mm_mark = {}
                sp_mark = {}
                st_mark = {}
                um_mark = {}
                vchunk = {}
                scan_marks = []
                hc_marks = []

                def issue_loads(ci):
                    b, q4 = divmod(ci, 4)
                    par = ci % 2
                    c0 = 2 * q4
                    if ci >= 2 and (ci - 2) in vchunk:
                        Q.wait_ge(svv.sem, vchunk[ci - 2])
                        Q.wait_ge(smm.sem, mm_mark[ci - 2])
                        S.wait_ge(svv.sem, vchunk[ci - 2])
                    for cx in range(2):
                        c = c0 + cx
                        dma("q", x2ch[par][:, :, cx * 512:(cx + 1) * 512],
                            a1_out[b][c, :, :].rearrange("(c2 p) t -> p c2 t", p=128))
                        dma("q", dtc[par][:, :, cx * 512:(cx + 1) * 512],
                            ag_out[b][c, 0:DTR, :].rearrange("(c2 p) t -> p c2 t", p=128))
                    Q.wait_ge(dq.sem, dq.val)
                    for n in range(NST):
                        dma("s", bq[par][:, n, :].rearrange("p (c t) -> p c t", c=2),
                            ag_out[b][c0:c0 + 2, DTR + n, :].partition_broadcast(128))
                    S.wait_ge(ds.sem, ds.val)
                    ld_mark[ci] = (dq.val, ds.val)

                def issue_cq(ci):
                    b, q4 = divmod(ci, 4)
                    c0 = 2 * q4
                    for n in range(NST):
                        dma("q", cq[:, n, :].rearrange("p (c t) -> p c t", c=2),
                            ag_out[b][c0:c0 + 2, DTR + NST + n, :].partition_broadcast(128))
                    Q.wait_ge(dq.sem, dq.val)
                    cq_mark[ci] = dq.val

                def issue_delta_mm(ci):
                    par = ci % 2
                    T.wait_ge(dq.sem, ld_mark[ci][0])
                    if ci >= 2:
                        T.wait_ge(svv.sem, um_mark[ci - 2])
                        T.wait_ge(sss.sem, sp_mark[ci - 2])
                    for ot in range(2):
                        for ch2 in range(2):
                            psDo = ps8[:, par * 4 + ot * 2 + ch2, :]
                            for kt in range(2):
                                mi = T.matmul(psDo, wdt[:, kt, ot * 128:(ot + 1) * 128],
                                              dtc[par][:, kt, ch2 * 512:(ch2 + 1) * 512],
                                              start=(kt == 0), stop=(kt == 1))
                    imark(smm, mi)
                    mm_mark[ci] = smm.val

                def issue_softplus(ci):
                    par = ci % 2
                    psD = ps8[:, par * 4:par * 4 + 4, :].rearrange("p (o c) t -> p o c t", o=2)
                    S.wait_ge(smm.sem, mm_mark[ci])
                    if ci >= 2:
                        S.wait_ge(svv.sem, vchunk.get(ci - 2, 0))
                    for ot in range(2):
                        mi = S.activation(dlt[par][:, ot, :], psD[:, ot, :, :], AF.Exp,
                                          scale=-1.0, bias=nbdt[:, ot:ot + 1])
                    schain(sss, S, mi)
                    imark(sss, S.activation(dlt[par][:], dlt[par][:], AF.Ln, bias=1.0))
                    sp_mark[ci] = sss.val

                issue_loads(0)
                issue_cq(0)
                issue_delta_mm(0)
                issue_softplus(0)

                it = 0
                for ci in range(NCH):
                    b, q4 = divmod(ci, 4)
                    par = ci % 2
                    psD = ps8[:, par * 4:par * 4 + 4, :].rearrange("p (o c) t -> p o c t", o=2)
                    if ci + 1 < NCH:
                        issue_loads(ci + 1)
                        issue_delta_mm(ci + 1)
                    V.wait_ge(sss.sem, sp_mark[ci])
                    V.wait_ge(dq.sem, ld_mark[ci][0])
                    V.wait_ge(ds.sem, ld_mark[ci][1])
                    for ot in range(2):
                        mi = V.scalar_tensor_tensor(dlt[par][:, ot, :], psD[:, ot, :, :],
                                                    bdt[:, ot:ot + 1], dlt[par][:, ot, :],
                                                    ALU.add, ALU.add)
                    schain(svv, V, mi)
                    schain(svv, V, V.tensor_tensor(x2ch[par][:], dlt[par][:],
                                                   x2ch[par][:], ALU.mult))
                    um_mark[ci] = svv.val
                    for dc in range(2):
                        for n in range(NST):
                            k = it
                            kp = k % 2
                            if dc == 0 and n == 0:
                                S.wait_ge(svv.sem, um_mark[ci])
                            if k >= 2:
                                S.wait_ge(svv.sem, scan_marks[k - 2])
                            imark(sss, S.activation(dab[kp][:], dlt[par][:, dc, :], AF.Exp,
                                                    scale=aneg[:, dc, n:n + 1]))
                            if k >= 1:
                                V.wait_ge(svv.sem, scan_marks[k - 1])
                            schain(svv, V, V.tensor_tensor(dbu[:], x2ch[par][:, dc, :],
                                                           bq[par][:, n, :], ALU.mult))
                            V.wait_ge(sss.sem, sss.val)
                            if k >= 2:
                                V.wait_ge(sgg.sem, hc_marks[k - 2])
                            if q4 == 0:
                                mi = V.tensor_tensor_scan(hbt[kp][:], dab[kp][:], dbu[:], 0.0,
                                                          ALU.mult, ALU.add)
                            else:
                                mi = V.tensor_tensor_scan(hbt[kp][:], dab[kp][:], dbu[:],
                                                          stt[:, dc * 8 + n:dc * 8 + n + 1],
                                                          ALU.mult, ALU.add)
                            if q4 != 3:
                                schain(svv, V, mi)
                                mi = V.tensor_copy(stt[:, dc * 8 + n:dc * 8 + n + 1],
                                                   hbt[kp][:, QT - 1:QT])
                            imark(svv, mi)
                            scan_marks.append(svv.val)
                            if n == 0 and dc == 0:
                                G.wait_ge(dq.sem, cq_mark[ci])
                            G.wait_ge(svv.sem, scan_marks[-1])
                            imark(sgg, G.tensor_tensor(hcq[:, :, n], hbt[kp][:],
                                                       cq[:, n, :], ALU.mult))
                            hc_marks.append(sgg.val)
                            it += 1
                        if dc == 0 and ci + 1 < NCH:
                            issue_softplus(ci + 1)
                        V.wait_ge(sgg.sem, hc_marks[-1])
                        if dc == 0 and ci >= 1 and (ci - 1) in st_mark:
                            V.wait_ge(dq.sem, st_mark[ci - 1])
                        imark(svv, V.tensor_reduce(ysc[:, dc, :], hcq[:], AX.X, ALU.add))
                        G.wait_ge(svv.sem, svv.val)
                    vchunk[ci] = svv.val
                    if ci + 1 < NCH:
                        Q.wait_ge(sgg.sem, hc_marks[-1])
                        issue_cq(ci + 1)
                    c0 = 2 * q4
                    Q.wait_ge(svv.sem, vchunk[ci])
                    for cx in range(2):
                        c = c0 + cx
                        dma("q", a2_in[b][c, :, :].rearrange("(c2 p) t -> p c2 t", p=128),
                            ysc[:, :, cx * 512:(cx + 1) * 512])
                    Q.wait_ge(dq.sem, dq.val)
                    st_mark[ci] = dq.val
                bar()

                if TRUNC >= 6:
                    imark(scc, G.collective_compute("AllToAll", ALU.bypass,
                                                    ins=[a2_inT], outs=[a2_outT],
                                                    replica_groups=[list(range(NC))]))
                G.wait_ge(scc.sem, scc.val)
                bar()

        es3 = ExitStack()
        sb3 = lambda n, s, d=F32: es3.enter_context(nc.sbuf_tensor(n, s, d))
        with es3:
            ytl = sb3("ytl", [128, DC, TOK])
            tn3 = dict(
                wbh=sb3("wbh3", [128, DC, 512], BF16),
                wb2=None,
                xlnb=sb3("xlnb3", [128, DC, TPC], BF16),
                mtok=sb3("mtok3", [128, LT, 4, NST], BF16),
                m8st=sb3("m8st3", [128, 4, 64]),
                mbuf=sb3("mbuf3", [128, TOK], BF16),
                hmk=sb3("hmk3", [128, LT, TOK], BF16),
                mx8=sb3("mx83", [128, 1]),
                logsb=sb3("logsb3", [128, NST]),
            )
            tn3["wbhB"] = sb3("wbhB3", [128, DC, 512], BF16)
            tn3["at"] = sb3("at3", [128, DC, 128], BF16)
            tn3["bw"] = sb3("bw3", [128, D], BF16)
            for nm in ("mbc", "rbc", "vtm", "scr"):
                lnt[nm] = sb3(nm + "3", [128, TOK])
            tn3["wb2"] = [tn3["wbh"], tn3["wbhB"]]

            with nc.Block():
                dma("g", tn3["at"][:], at_d.rearrange("c p e -> p c e"))
                dma("g", tn3["bw"][:], bw_d)
                for j in (range(LT) if TRUNC >= 7 else []):
                    b = j
                    for g8 in range(NC):
                        dma("q", ytl[:, 2 * g8:2 * g8 + 2, :],
                            a2_out[b][g8, :, :].rearrange("(c2 p) t -> p c2 t", p=128))
                    V.wait_ge(dq.sem, dq.val)
                    imark(svv, V.tensor_tensor(X[:, :, j * TOK:(j + 1) * TOK],
                                               X[:, :, j * TOK:(j + 1) * TOK],
                                               ytl[:], ALU.add))
                    Q.wait_ge(svv.sem, svv.val)
                bar()
                dma("g", tn3["wbh"][:],
                    wb_d.rearrange("c p o -> p c o")[:, :, 0:512])
                bar()
                if TRUNC >= 8:
                    ffn_sublayer(tn3)
                dma("q", out_d.rearrange("c p t -> p c t"), X[:])
                for eng in ENG.values():
                    eng.wait_ge(dq.sem, dq.val)

    return nc


_NC_CACHE = {}


def kernel(L0_attn_x, L0_attn_W_xproj, L0_attn_W_dt, L0_attn_b_dt, L0_attn_A_log,
           L0_attn_D_param, L0_ffn_W_base, L0_ffn_b_base, L0_ffn_router_W,
           L0_ffn_A, L0_ffn_Bw):
    x = np.ascontiguousarray(np.asarray(L0_attn_x, dtype=np.float32).reshape(NC * TPC, D).T)
    wbT = np.ascontiguousarray(np.asarray(L0_ffn_W_base, np.float32).T).reshape(DC, 128, D)
    bb = np.asarray(L0_ffn_b_base, np.float32).reshape(1, D)
    rtT = np.ascontiguousarray(np.asarray(L0_ffn_router_W, np.float32).T).reshape(DC, 128, NST)
    atT = np.ascontiguousarray(np.asarray(L0_ffn_A, np.float32).reshape(128, D).T).reshape(DC, 128, 128)
    bwT = np.ascontiguousarray(np.transpose(np.asarray(L0_ffn_Bw, np.float32), (0, 2, 1)).reshape(128, D))
    wxT = np.zeros((D, 384), np.float32)
    wxT[:, 0:272] = np.asarray(L0_attn_W_xproj, np.float32).T
    wxT = wxT.reshape(DC, 128, 384)
    ones_col = np.ones((128, 128), np.float32)
    ones_row = np.ones((1, TOK), np.float32)
    eye = np.eye(128, dtype=np.float32)
    e16 = np.zeros((NST, 128), np.float32)
    for e in range(NST):
        e16[e, e * 16:(e + 1) * 16] = 1.0
    dparv = np.ascontiguousarray(np.asarray(L0_attn_D_param, np.float32).reshape(DC, 128).T)
    W_dt = np.asarray(L0_attn_W_dt, np.float32)
    b_dt = np.asarray(L0_attn_b_dt, np.float32)
    A_log = np.asarray(L0_attn_A_log, np.float32)

    key = os.environ.get("KTRUNC", "9")
    if key not in _NC_CACHE:
        _NC_CACHE[key] = build_program()
    nc = _NC_CACHE[key]

    in_maps = []
    for c in range(NC):
        cols = np.concatenate([np.arange((8 * j + c) * TOK, (8 * j + c + 1) * TOK)
                               for j in range(LT)])
        x0c = np.ascontiguousarray(x[:, cols]).reshape(DC, 128, TPC)
        wdtT = np.ascontiguousarray(W_dt[CH * c:CH * (c + 1), :].T).reshape(2, 128, CH)
        in_maps.append(dict(
            x0=x0c, wbT=wbT, bb=bb, routerT=rtT, AT=atT, BwT=bwT, WxpT=wxT,
            WdtT=wdtT,
            bdt=np.ascontiguousarray(b_dt[CH * c:CH * (c + 1)].reshape(2, 128).T),
            Alog=np.ascontiguousarray(
                A_log[CH * c:CH * (c + 1), :].reshape(2, 128, NST).transpose(1, 0, 2)),
            Dpar=dparv, ones_col=ones_col, ones_row=ones_row, eye=eye, E16=e16,
        ))
    ncore = int(os.environ.get("KCORES", str(NC)))
    res = run_bass_kernel_spmd(nc, in_maps[:ncore], core_ids=list(range(ncore)))
    out = np.zeros((D, NC * TPC), np.float32)
    for c in range(ncore):
        cols = np.concatenate([np.arange((8 * j + c) * TOK, (8 * j + c + 1) * TOK)
                               for j in range(LT)])
        out[:, cols] = res.results[c]["out"].reshape(D, TPC)
    return np.ascontiguousarray(out.T).reshape(2, 4096, D)
